# version 06
# baseline (speedup 1.0000x reference)
import contextlib

import numpy as np

import concourse.bass as bass
import concourse.mybir as mybir
from concourse.bass_utils import run_bass_kernel_spmd

F32 = mybir.dt.float32
I32 = mybir.dt.int32
AF = mybir.ActivationFunctionType
ALU = mybir.AluOpType
AX = mybir.AxisListType

D = 1024
NCH = 8
H0 = 4096
H1 = 2048
NZ = 8
NE = 8
RH = 256

C_RNE = 1.5 * 2 ** 23
PI_LO = float(np.nextafter(np.float32(np.pi), np.float32(0)))
INV2PI = float(np.float32(1.0 / (2 * np.pi)))
RC1 = 6.28125
RC2 = float(np.float32(2 * np.pi - RC1))
RC3 = float(np.float32(2 * np.pi - RC1 - np.float64(np.float32(2 * np.pi - RC1))))
P2H = float(np.float32(np.pi / 2))
P2L = float(np.float32(np.pi / 2 - np.float64(np.float32(np.pi / 2))))
LOG2E = float(np.float32(1.4426950408889634))
LN2 = float(np.float32(0.6931471805599453))

ENGS = ("sync", "gpsimd", "tensor", "vector", "scalar")
DMA_SEMS = ("sq", "gq")
ALL_SEMS = ("pe", "dve", "act", "gc", "sq", "gq", "cc")


class Prog:
    def __init__(self, nc):
        self.nc = nc
        self.steps = []
        self.counts = {s: 0 for s in ALL_SEMS}
        self.sems = {}
        self.pending = {e: () for e in ENGS}
        self.final_waits = {}

    def step(self, eng, fn, waits=(), signal=None):
        waits = self.pending[eng] + tuple(w for w in waits if w is not None)
        self.pending[eng] = ()
        waits = tuple(("gq", self.counts["gq"]) if s == "gq" else (s, v)
                      for (s, v) in waits)
        ev = None
        if signal is not None:
            self.counts[signal] += 16 if signal in DMA_SEMS else 1
            ev = (signal, self.counts[signal])
        self.steps.append((eng, waits, fn, signal))
        return ev

    def dma(self, queue, out, in_, waits=(), slow=False):
        eng = "sync" if queue == "sq" else "gpsimd"
        return self.step(
            eng, lambda e, o=out, i=in_, sl=slow: e.dma_start(
                out=o, in_=i, allow_slow_non_contiguous=sl), waits, queue)

    def pe(self, fn, waits=()):
        return self.step("tensor", fn, waits, "pe")

    def dve(self, fn, waits=()):
        return self.step("vector", fn, waits, "dve")

    def act(self, fn, waits=()):
        return self.step("scalar", fn, waits, "act")

    def gpc(self, fn, waits=()):
        return self.step("gpsimd", fn, waits, "gc")

    def barrier(self):
        waits = tuple((sm, self.counts[sm]) for sm in ALL_SEMS
                      if self.counts[sm] > 0)
        for eng in ENGS:
            self.pending[eng] = waits
        return waits

    def emit(self, block):
        sems = self.sems
        for engname in ENGS:
            def make(en):
                def body(eng):
                    for (e, waits, fn, signal) in self.steps:
                        if e != en:
                            continue
                        for (s, v) in waits:
                            eng.wait_ge(sems[s], v)
                        instr = fn(eng)
                        if signal is not None:
                            instr.then_inc(sems[signal],
                                           16 if signal in DMA_SEMS else 1)
                    for (s, v) in self.final_waits.get(en, ()):
                        eng.wait_ge(sems[s], v)
                return body
            getattr(block, engname)(make(engname))


def build_program(T, n_pairs=4, nsub=6, subsel=None, out_src="xfm", dbg=False, dbg2=False, dbg3=False):
    TT = min(512, T)
    NTT = T // TT
    TK = 2 * T
    NKT = TK // 128
    NJT = T // 128

    nc = bass.Bass("TRN2", target_bir_lowering=False, debug=False,
                   detect_race_conditions=False)
    dr = {}

    def din(name, shape, dt=F32):
        dr[name] = nc.dram_tensor(name, shape, dt, kind="ExternalInput").ap()

    def dint(name, shape, dt=F32):
        dr[name] = nc.dram_tensor(name, shape, dt).ap()

    din("x_in", [T, D])
    din("W_qkv0", [D, 3 * D]); din("b_qkv0", [3 * D])
    din("W_out0", [D, D]); din("b_out0", [D])
    din("W1r", [D, RH]); din("b1r", [RH]); din("W2r", [RH, NZ]); din("b2r", [NZ])
    din("Wa", [NZ, D, H0]); din("Wb", [NZ, H0, D])
    din("W_qkv1", [D, 3 * D]); din("b_qkv1", [3 * D])
    din("W_out1", [D, D]); din("b_out1", [D])
    din("rt_w", [D, NE]); din("rt_b", [NE])
    din("sw1", [D, H1]); din("sb1", [H1]); din("sw2", [H1, D]); din("sb2", [D])
    din("rw1", [NE, D, H1]); din("rb1", [NE, H1])
    din("rw2", [NE, H1, D]); din("rb2", [NE, D])
    din("c_ident", [128, 128]); din("c_ones", [128, 128])
    din("c_permT", [128, 128]); din("c_toff", [1, 1])
    dr["y"] = nc.dram_tensor("y", [T, D], F32, kind="ExternalOutput").ap()

    dint("xfm", [D, T])
    dint("xn_dram", [D, T])
    dint("qq", [D, T])
    dint("oo", [D, T])
    dint("kq_dram", [D, 3 * D])
    dint("ko_dram", [D, D])
    dint("kv_snd", [1, 2 * D * T])
    dint("kv_all", [2, 2 * D * T])
    dint("pp_snd", [1, D])
    dint("pp_all", [2, D])

    REPL = [[2 * i, 2 * i + 1] for i in range(n_pairs)]
    p = Prog(nc)
    es = contextlib.ExitStack()

    def kvr(t):
        return t.rearrange("r (g c p t) -> r g c p t", g=2, c=NCH, p=128, t=T)

    def fmv(t):
        return t.rearrange("(c p) t -> p c t", p=128)

    with es:
        A1 = es.enter_context(nc.sbuf_tensor("A1", [128, max(8 * T, 32 * TT)], F32))
        A2 = es.enter_context(nc.sbuf_tensor("A2", [128, 8192], F32))
        A3 = es.enter_context(nc.sbuf_tensor("A3", [128, 4096], F32))
        A4 = es.enter_context(nc.sbuf_tensor("A4", [128, 4096], F32))
        A5 = es.enter_context(nc.sbuf_tensor("A5", [128, 8 * TT], F32))
        A6 = es.enter_context(nc.sbuf_tensor("A6", [128, 8 * TT], F32))
        A7 = es.enter_context(nc.sbuf_tensor("A7", [128, 4 * TT], F32))
        rows = es.enter_context(nc.sbuf_tensor("rows", [1, 2 * T], F32))
        smalls = es.enter_context(nc.sbuf_tensor("smalls", [128, 256], F32))
        consts = es.enter_context(nc.sbuf_tensor("consts", [128, 3 * 128], F32))
        tiny = es.enter_context(nc.sbuf_tensor("tiny", [128, 256], F32))
        rb2sb = es.enter_context(nc.sbuf_tensor("rb2sb", [8, D], F32))
        rwsb = es.enter_context(nc.sbuf_tensor("rwsb", [128, 8 * NE], F32))
        rbsb = es.enter_context(nc.sbuf_tensor("rbsb", [8, 1], F32))
        wfm = es.enter_context(nc.sbuf_tensor("wfm", [8, T], F32))
        ii32 = es.enter_context(nc.sbuf_tensor("ii32", [1, 128], I32))
        ps = [es.enter_context(nc.psum_tensor("ps%d" % i, [128, 512], F32))
              for i in range(8)]
        for s in ALL_SEMS:
            p.sems[s] = es.enter_context(nc.semaphore("sem_" + s))
        block = es.enter_context(nc.Block())

        ident = consts[:, 0:128]
        ones = consts[:, 128:256]
        permT = consts[:, 256:384]
        WACOLS = NCH * 256
        WBCOLS = (H0 // 128) * 128
        BQKV0, BOUT0, SB1, SB2, RB1, BQKV1, BOUT1, TOFF = \
            0, 24, 32, 48, 49, 177, 201, 210

        p.dma("gq", consts[:, 0:128], dr["c_ident"])
        p.dma("gq", consts[:, 128:256], dr["c_ones"])
        p.dma("gq", consts[:, 256:384], dr["c_permT"])
        p.dma("gq", smalls[0:1, TOFF:TOFF + 1], dr["c_toff"])
        p.dma("gq", smalls[:, BQKV0:BQKV0 + 24],
              dr["b_qkv0"].rearrange("(c p) -> p c", p=128), slow=True)
        p.dma("gq", smalls[:, BOUT0:BOUT0 + 8],
              dr["b_out0"].rearrange("(c p) -> p c", p=128), slow=True)
        p.dma("gq", smalls[:, SB1:SB1 + 16],
              dr["sb1"].rearrange("(c p) -> p c", p=128), slow=True)
        p.dma("gq", smalls[:, SB2:SB2 + 1],
              dr["sb2"].rearrange("(c p) -> p c", p=128)[:, 0:1], slow=True)
        p.dma("gq", smalls[:, SB2 + 1:SB2 + 8],
              dr["sb2"].rearrange("(c p) -> p c", p=128)[:, 1:8], slow=True)
        for e_ in range(NE):
            p.dma("gq", smalls[:, RB1 + e_ * 16:RB1 + e_ * 16 + 16],
                  dr["rb1"].rearrange("e (c p) -> p e c", p=128)[:, e_],
                  slow=True)
        p.dma("gq", smalls[:, BQKV1:BQKV1 + 24],
              dr["b_qkv1"].rearrange("(c p) -> p c", p=128), slow=True)
        p.dma("gq", smalls[:, BOUT1:BOUT1 + 8],
              dr["b_out1"].rearrange("(c p) -> p c", p=128), slow=True)
        p.dma("gq", rb2sb[:, :], dr["rb2"])
        p.dma("gq", rwsb[:, :].rearrange("p (c n) -> p c n", n=NE),
              dr["rt_w"].rearrange("(c p) n -> p c n", p=128), slow=True)
        p.dma("gq", rbsb[:, :], dr["rt_b"].rearrange("(e o) -> e o", o=1), slow=True)
        p.dve(lambda e: e.memset(tiny[:, 95:96], 1e-5))
        p.barrier()

        evv_h = {}
        st_h = {}
        for jt in range(NJT):
            evl = p.dma("gq", A1[:, 0:D], dr["x_in"][jt * 128:(jt + 1) * 128, :],
                        (evv_h.get(jt * NCH - 1),))
            for c in range(NCH):
                bk = c % 2
                g = jt * NCH + c
                evt = p.pe(lambda e, c=c, bk=bk: e.transpose(
                    ps[bk][:, 0:128], A1[:, c * 128:(c + 1) * 128], ident),
                    (evl, evv_h.get(g - 2)))
                evv = p.dve(lambda e, c=c, bk=bk: e.tensor_copy(
                    A7[:, (c % 4) * 128:(c % 4) * 128 + 128], ps[bk][:, 0:128]),
                    (evt, st_h.get(g - 4)))
                evv_h[g] = evv
                st_h[g] = p.dma("gq", dr["xfm"][c * 128:(c + 1) * 128,
                                                jt * 128:(jt + 1) * 128],
                                A7[:, (c % 4) * 128:(c % 4) * 128 + 128], (evv,))
        p.barrier()

        def emit_ln(pooled):
            if pooled:
                p.dve(lambda e: e.memset(tiny[:, 0:8], 0.0))
            prev_xn = None
            for tt in range(NTT):
                lo = tt * TT
                lds = [p.dma("gq", A5[:, c * TT:(c + 1) * TT],
                             dr["xfm"][c * 128:(c + 1) * 128, lo:lo + TT],
                             (prev_xn,))
                       for c in range(NCH)]

                def s1(e, lds=lds):
                    for c in range(NCH):
                        r = e.matmul(ps[0][0:1, 0:TT], ones[:, 0:1],
                                     A5[:, c * TT:(c + 1) * TT],
                                     start=(c == 0), stop=(c == NCH - 1))
                    return r
                ev_s1 = p.pe(s1, tuple(lds))
                ev_m = p.act(lambda e, tt=tt: e.activation(
                    rows[0:1, tt * TT:tt * TT + TT], ps[0][0:1, 0:TT],
                    AF.Identity, scale=1.0 / D), (ev_s1,))
                ev_mb = p.pe(lambda e, tt=tt: e.matmul(
                    ps[1][:, 0:TT], ones[0:1, :],
                    rows[0:1, tt * TT:tt * TT + TT], start=True, stop=True),
                    (ev_m,))
                last_s2 = None
                for c in range(NCH):
                    ev_sub = p.dve(lambda e, c=c: e.tensor_tensor(
                        A5[:, c * TT:(c + 1) * TT], A5[:, c * TT:(c + 1) * TT],
                        ps[1][:, 0:TT], ALU.subtract), (ev_mb,))
                    ev_sq = p.act(lambda e, c=c: e.activation(
                        A7[:, (c % 2) * TT:(c % 2) * TT + TT],
                        A5[:, c * TT:(c + 1) * TT], AF.Square),
                        (ev_sub, last_s2))
                    last_s2 = p.pe(lambda e, c=c: e.matmul(
                        ps[2][0:1, 0:TT], ones[:, 0:1],
                        A7[:, (c % 2) * TT:(c % 2) * TT + TT],
                        start=(c == 0), stop=(c == NCH - 1)), (ev_sq,))
                ev_sd = p.act(lambda e, tt=tt: e.activation(
                    rows[0:1, T + tt * TT:T + tt * TT + TT],
                    ps[2][0:1, 0:TT], AF.Sqrt, bias=tiny[0:1, 95:96],
                    scale=1.0 / D),
                    (last_s2,))
                ev_r = p.dve(lambda e, tt=tt: e.reciprocal(
                    rows[0:1, T + tt * TT:T + tt * TT + TT],
                    rows[0:1, T + tt * TT:T + tt * TT + TT]), (ev_sd,))
                ev_rb = p.pe(lambda e, tt=tt: e.matmul(
                    ps[3][:, 0:TT], ones[0:1, :],
                    rows[0:1, T + tt * TT:T + tt * TT + TT],
                    start=True, stop=True), (ev_r,))
                ev_xn = None
                for c in range(NCH):
                    ev_xn = p.dve(lambda e, c=c: e.tensor_tensor(
                        A6[:, c * TT:(c + 1) * TT], A5[:, c * TT:(c + 1) * TT],
                        ps[3][:, 0:TT], ALU.mult), (ev_rb,))
                    if pooled:
                        ev_red = p.dve(lambda e, c=c: e.tensor_reduce(
                            tiny[:, 8 + c:9 + c], A6[:, c * TT:(c + 1) * TT],
                            AX.X, ALU.add), (ev_xn,))
                        p.dve(lambda e, c=c: e.tensor_tensor(
                            tiny[:, c:c + 1], tiny[:, c:c + 1],
                            tiny[:, 8 + c:9 + c], ALU.add), (ev_red,))
                p.dma("gq", fmv(dr["xn_dram"])[:, :, lo:lo + TT],
                      A6[:, :].rearrange("p (c t) -> p c t", t=TT), (ev_xn,))
                prev_xn = ev_xn
            p.barrier()

        emit_l0_ffn_router_only = [False]

        def emit_l0_ffn():
            emit_ln(pooled=True)
            ev_ps = p.dma("gq", dr["pp_snd"].rearrange("o (c p) -> p (o c)", p=128),
                          tiny[:, 0:8], slow=True)
            ev_cc = p.step("gpsimd", lambda e: e.collective_compute(
                "AllGather", ALU.bypass, replica_groups=REPL,
                ins=[dr["pp_snd"]], outs=[dr["pp_all"]]), (ev_ps,), "cc")
            ev_p0 = p.dma("gq", tiny[:, 8:16],
                          dr["pp_all"].rearrange("r (c p) -> p (r c)", p=128)[:, 0:8],
                          (ev_cc,), slow=True)
            ev_p1 = p.dma("gq", tiny[:, 16:24],
                          dr["pp_all"].rearrange("r (c p) -> p (r c)", p=128)[:, 8:16],
                          (ev_cc,), slow=True)
            ev = p.dve(lambda e: e.tensor_tensor(
                tiny[:, 24:32], tiny[:, 8:16], tiny[:, 16:24], ALU.add),
                (ev_p0, ev_p1))
            ev = p.dve(lambda e: e.tensor_scalar(
                tiny[:, 24:32], tiny[:, 24:32], 1.0 / (2 * T), None, ALU.mult),
                (ev,))
            evrw = p.dma("gq", A3[:, 0:2048].rearrange("p (c n) -> p c n", n=RH),
                         dr["W1r"].rearrange("(c p) n -> p c n", p=128),
                         slow=True)
            evrw2 = p.dma("gq", A3[:, 2048:2064].rearrange("p (c n) -> p c n", n=NZ),
                          dr["W2r"].rearrange("(c p) n -> p c n", p=128),
                          slow=True)
            evrb1 = p.dma("gq", A3[:, 2064:2066],
                          dr["b1r"].rearrange("(c p) -> p c", p=128), slow=True)
            evrb2 = p.dma("gq", A3[0:8, 2066:2067],
                          dr["b2r"].rearrange("(e o) -> e o", o=1), slow=True)
            for n in range(2):
                def mm1(e, n=n):
                    for c in range(NCH):
                        r = e.matmul(ps[0][:, n:n + 1],
                                     A3[:, c * RH + n * 128:c * RH + n * 128 + 128],
                                     tiny[:, 24 + c:25 + c],
                                     start=(c == 0), stop=(c == NCH - 1))
                    return r
                ev_h = p.pe(mm1, (ev, evrw, evrb1))
                ev_t = p.act(lambda e, n=n: e.activation(
                    tiny[:, 32 + n:33 + n], ps[0][:, n:n + 1], AF.Tanh,
                    bias=A3[:, 2064 + n:2065 + n]), (ev_h,))

            def mm2(e):
                for n in range(2):
                    r = e.matmul(ps[1][0:8, 0:1],
                                 A3[:, 2048 + n * NZ:2048 + (n + 1) * NZ],
                                 tiny[:, 32 + n:33 + n],
                                 start=(n == 0), stop=(n == 1))
                return r
            ev_lg = p.pe(mm2, (ev_t, evrw2))
            ev = p.act(lambda e: e.activation(
                tiny[0:8, 34:35], ps[1][0:8, 0:1], AF.Identity,
                bias=A3[0:8, 2066:2067]), (ev_lg, evrb2))
            ev = p.pe(lambda e: e.transpose(
                ps[2][0:1, 0:8], tiny[0:8, 34:35], ident[0:8, 0:8]), (ev,))
            ev = p.dve(lambda e: e.tensor_copy(tiny[0:1, 36:44], ps[2][0:1, 0:8]),
                       (ev,))
            ev = p.dve(lambda e: e.tensor_reduce(
                tiny[0:1, 44:45], tiny[0:1, 36:44], AX.X, ALU.max), (ev,))
            ev = p.dve(lambda e: e.tensor_scalar(
                tiny[0:1, 45:46], tiny[0:1, 44:45], -1.0, None, ALU.mult), (ev,))
            ev = p.act(lambda e: e.activation(
                tiny[0:1, 36:44], tiny[0:1, 36:44], AF.Exp,
                bias=tiny[0:1, 45:46]), (ev,))
            ev = p.dve(lambda e: e.tensor_reduce(
                tiny[0:1, 44:45], tiny[0:1, 36:44], AX.X, ALU.add), (ev,))
            ev = p.dve(lambda e: e.reciprocal(
                tiny[0:1, 44:45], tiny[0:1, 44:45]), (ev,))
            ev = p.dve(lambda e: e.tensor_scalar(
                tiny[0:1, 36:44], tiny[0:1, 36:44], tiny[0:1, 44:45], None,
                ALU.mult), (ev,))
            ev = p.dve(lambda e: e.tensor_reduce(
                tiny[0:1, 44:45], tiny[0:1, 36:44], AX.X, ALU.max), (ev,))
            ev = p.dve(lambda e: e.scalar_tensor_tensor(
                tiny[0:1, 46:54], tiny[0:1, 36:44], tiny[0:1, 44:45],
                tiny[0:1, 36:44], ALU.is_lt, ALU.mult), (ev,))
            ev = p.dve(lambda e: e.tensor_reduce(
                tiny[0:1, 44:45], tiny[0:1, 46:54], AX.X, ALU.max), (ev,))
            ev = p.dve(lambda e: e.scalar_tensor_tensor(
                tiny[0:1, 54:62], tiny[0:1, 46:54], tiny[0:1, 44:45],
                tiny[0:1, 46:54], ALU.is_lt, ALU.mult), (ev,))
            ev = p.dve(lambda e: e.tensor_reduce(
                tiny[0:1, 44:45], tiny[0:1, 54:62], AX.X, ALU.max), (ev,))
            ev = p.dve(lambda e: e.tensor_scalar(
                tiny[0:1, 46:54], tiny[0:1, 36:44], tiny[0:1, 44:45], None,
                ALU.is_ge), (ev,))
            ev = p.dve(lambda e: e.tensor_tensor(
                tiny[0:1, 46:54], tiny[0:1, 46:54], tiny[0:1, 36:44], ALU.mult),
                (ev,))
            ev = p.dve(lambda e: e.tensor_reduce(
                tiny[0:1, 44:45], tiny[0:1, 46:54], AX.X, ALU.add), (ev,))
            ev = p.dve(lambda e: e.reciprocal(
                tiny[0:1, 44:45], tiny[0:1, 44:45]), (ev,))
            ev = p.dve(lambda e: e.tensor_scalar(
                tiny[0:1, 46:54], tiny[0:1, 46:54], tiny[0:1, 44:45], None,
                ALU.mult), (ev,))
            ev = p.pe(lambda e: e.matmul(
                ps[2][:, 0:8], ones[0:1, :], tiny[0:1, 46:54],
                start=True, stop=True), (ev,))
            p.dve(lambda e: e.tensor_copy(tiny[:, 56:64], ps[2][:, 0:8]), (ev,))
            p.barrier()
            if emit_l0_ffn_router_only[0]:
                return

            NNB = H0 // 256
            for tt in range(NTT):
                lo = tt * TT
                ev_xn = p.dma("gq", A6[:, :].rearrange("p (c t) -> p c t", t=TT),
                              fmv(dr["xn_dram"])[:, :, lo:lo + TT])
                ev_x = p.dma("gq", A5[:, :].rearrange("p (c t) -> p c t", t=TT),
                             fmv(dr["xfm"])[:, :, lo:lo + TT])
                ev_z = {}
                gelu_evs = [None] * (H0 // 128)
                g1_evs = {}
                g2_evs = {}
                for e_i in range(NZ):
                    for nb in range(NNB):
                        off = (nb % 2) * WACOLS
                        if (e_i, nb - 2) in g1_evs:
                            ww = (g1_evs[(e_i, nb - 2)],)
                        elif e_i > 0:
                            ww = (g1_evs[(e_i - 1, NNB - 2 + nb)],)
                        else:
                            ww = (ev_xn,)
                        evw = p.dma("sq",
                                    A3[:, off:off + WACOLS].rearrange(
                                        "p (c n) -> p c n", n=256),
                                    dr["Wa"][e_i].rearrange(
                                        "(c p) n -> p c n",
                                        p=128)[:, :, nb * 256:(nb + 1) * 256],
                                    ww)
                        for nn in range(2):
                            n = nb * 2 + nn
                            pbk = 4 + n % 2

                            def g1(e, nn=nn, off=off, pbk=pbk):
                                for c in range(NCH):
                                    r = e.matmul(
                                        ps[pbk][:, 0:TT],
                                        A3[:, off + c * 256 + nn * 128:
                                           off + c * 256 + nn * 128 + 128],
                                        A6[:, c * TT:(c + 1) * TT],
                                        start=(c == 0), stop=(c == NCH - 1))
                                return r
                            evg = p.pe(g1, (evw, ev_xn, gelu_evs[n - 2]))
                            g1_evs[(e_i, nb)] = evg
                            gelu_evs[n] = p.act(
                                lambda e, n=n, pbk=pbk: e.activation(
                                    A1[:, n * TT:(n + 1) * TT], ps[pbk][:, 0:TT],
                                    AF.Gelu_apprx_tanh), (evg,))
                    for d_i in range(8):
                        off = (d_i % 2) * WBCOLS
                        if (e_i, d_i - 2) in g2_evs:
                            ww = (g2_evs[(e_i, d_i - 2)],)
                        elif e_i > 0:
                            ww = (g2_evs[(e_i - 1, 6 + d_i)],)
                        else:
                            ww = ()
                        evw = p.dma("sq",
                                    A2[:, off:off + WBCOLS].rearrange(
                                        "p (c n) -> p c n", n=128),
                                    dr["Wb"][e_i].rearrange(
                                        "(c p) n -> p c n",
                                        p=128)[:, :, d_i * 128:(d_i + 1) * 128],
                                    ww)
                        pbk = 6 + d_i % 2

                        def g2(e, off=off, pbk=pbk):
                            for n2 in range(H0 // 128):
                                r = e.matmul(
                                    ps[pbk][:, 0:TT],
                                    A2[:, off + n2 * 128:off + n2 * 128 + 128],
                                    A1[:, n2 * TT:(n2 + 1) * TT],
                                    start=(n2 == 0), stop=(n2 == H0 // 128 - 1))
                            return r
                        evg2 = p.pe(g2, (evw, gelu_evs[H0 // 128 - 1],
                                         ev_z.get(d_i - 2),
                                         ev_z.get(6 + d_i) if e_i > 0 and d_i < 2
                                         else None))
                        g2_evs[(e_i, d_i)] = evg2
                        if e_i == 0:
                            ev_z[d_i] = p.dve(
                                lambda e, d=d_i, pbk=pbk, we=56 + e_i:
                                e.tensor_scalar(
                                    A4[:, d * TT:d * TT + TT], ps[pbk][:, 0:TT],
                                    tiny[:, we:we + 1], None, ALU.mult), (evg2,))
                        else:
                            ev_z[d_i] = p.dve(
                                lambda e, d=d_i, pbk=pbk, we=56 + e_i:
                                e.scalar_tensor_tensor(
                                    A4[:, d * TT:d * TT + TT], ps[pbk][:, 0:TT],
                                    tiny[:, we:we + 1],
                                    A4[:, d * TT:d * TT + TT],
                                    ALU.mult, ALU.add), (evg2, ev_z.get(d_i)))
                for c in range(NCH):
                    ev = p.dve(lambda e, c=c: e.tensor_tensor(
                        A4[:, c * TT:c * TT + TT], A4[:, c * TT:c * TT + TT],
                        A6[:, c * TT:(c + 1) * TT], ALU.add),
                        (ev_z[c], ev_xn))
                    ev = p.dve(lambda e, c=c: e.scalar_tensor_tensor(
                        A5[:, c * TT:(c + 1) * TT],
                        A4[:, c * TT:c * TT + TT], 0.5,
                        A5[:, c * TT:(c + 1) * TT], ALU.mult, ALU.add),
                        (ev, ev_x))
                    p.dma("gq", dr["xfm"][c * 128:(c + 1) * 128, lo:lo + TT],
                          A5[:, c * TT:(c + 1) * TT], (ev,))
                p.barrier()

        def emit_rope_tables():
            TH, KU, KA, MK = 0, T, 2 * T, 3 * T
            ev = None
            for jt in range(NJT):
                ev_i = p.gpc(lambda e, jt=jt: e.iota(
                    ii32[0:1, 0:128], [[1, 128]], base=jt * 128,
                    channel_multiplier=0), (ev,))
                ev = p.dve(lambda e, jt=jt: e.tensor_copy(
                    A1[0:1, 4 * T + jt * 128:4 * T + jt * 128 + 128],
                    ii32[0:1, 0:128]), (ev_i,))
            ev_t = p.dve(lambda e: e.tensor_scalar(
                A1[0:1, 4 * T:5 * T], A1[0:1, 4 * T:5 * T],
                smalls[0:1, TOFF:TOFF + 1], None, ALU.add), (ev,))
            IV = 128
            ev_i2 = p.gpc(lambda e: e.iota(ii32[0:1, 0:32], [[1, 32]], base=0,
                                           channel_multiplier=0), (ev,))
            ev_u = p.dve(lambda e: e.tensor_copy(tiny[0:1, IV + 64:IV + 96],
                                                 ii32[0:1, 0:32]), (ev_i2,))
            ev_c = p.dve(lambda e: e.memset(tiny[0:1, IV + 126:IV + 127], 10000.0))
            ev_l = p.act(lambda e: e.activation(tiny[0:1, IV + 125:IV + 126],
                                                tiny[0:1, IV + 126:IV + 127],
                                                AF.Ln), (ev_c,))
            ev_nl = p.dve(lambda e: e.tensor_scalar(
                tiny[0:1, IV + 124:IV + 125], tiny[0:1, IV + 125:IV + 126],
                -1.0 / 32.0, None, ALU.mult), (ev_l,))
            ev_f = p.act(lambda e: e.activation(
                tiny[0:1, IV + 64:IV + 96], tiny[0:1, IV + 64:IV + 96], AF.Exp,
                scale=tiny[0:1, IV + 124:IV + 125]), (ev_u, ev_nl))
            ev4 = None
            for rrep in [0, 1, 3]:
                ev4 = p.dve(lambda e, r=rrep: e.tensor_copy(
                    tiny[0:1, IV + r * 32:IV + r * 32 + 32],
                    tiny[0:1, IV + 64:IV + 96]), (ev_f,))
            evth = None
            for tt in range(NTT):
                evm = p.pe(lambda e, tt=tt: e.matmul(
                    ps[0][:, 0:TT], tiny[0:1, IV:IV + 128],
                    A1[0:1, 4 * T + tt * TT:4 * T + tt * TT + TT],
                    start=True, stop=True), (ev4, ev_t, evth))
                evth = p.dve(lambda e, tt=tt: e.tensor_copy(
                    A1[:, TH + tt * TT:TH + tt * TT + TT], ps[0][:, 0:TT]),
                    (evm,))
            ev = p.dve(lambda e: e.tensor_scalar(
                A1[:, KU:KU + T], A1[:, TH:TH + T], INV2PI, 0.5,
                ALU.mult, ALU.add), (evth,))
            ev = p.dve(lambda e: e.memset(A1[:, KA:KA + T], 0.0), (ev,))
            for bit in [512.0, 256.0, 128.0, 64.0, 32.0, 16.0, 8.0, 4.0, 2.0, 1.0]:
                ev = p.dve(lambda e, b=bit: e.tensor_scalar(
                    A1[:, MK:MK + T], A1[:, KU:KU + T], b, None, ALU.is_ge), (ev,))
                ev = p.dve(lambda e, b=bit: e.scalar_tensor_tensor(
                    A1[:, KU:KU + T], A1[:, MK:MK + T], -b,
                    A1[:, KU:KU + T], ALU.mult, ALU.add), (ev,))
                ev = p.dve(lambda e, b=bit: e.scalar_tensor_tensor(
                    A1[:, KA:KA + T], A1[:, MK:MK + T], b,
                    A1[:, KA:KA + T], ALU.mult, ALU.add), (ev,))
            for cc_ in (RC1, RC2, RC3):
                ev = p.dve(lambda e, cc2=cc_: e.scalar_tensor_tensor(
                    A1[:, TH:TH + T], A1[:, KA:KA + T], -cc2,
                    A1[:, TH:TH + T], ALU.mult, ALU.add), (ev,))
            ev = p.dve(lambda e: e.tensor_scalar(
                A1[:, TH:TH + T], A1[:, TH:TH + T], -PI_LO, None, ALU.max), (ev,))
            ev = p.dve(lambda e: e.tensor_scalar(
                A1[:, TH:TH + T], A1[:, TH:TH + T], PI_LO, None, ALU.min), (ev,))
            ev_sin = p.act(lambda e: e.activation(
                A2[:, 0:T], A1[:, TH:TH + T], AF.Sin), (ev,))
            ev = p.dve(lambda e: e.tensor_scalar(
                A1[:, KU:KU + T], A1[:, TH:TH + T], P2H, P2L, ALU.add, ALU.add),
                (ev,))
            ev = p.dve(lambda e: e.tensor_scalar(
                A1[:, MK:MK + T], A1[:, KU:KU + T], PI_LO, None, ALU.is_gt), (ev,))
            for cc_ in (RC1, RC2, RC3):
                ev = p.dve(lambda e, cc2=cc_: e.scalar_tensor_tensor(
                    A1[:, KU:KU + T], A1[:, MK:MK + T], -cc2,
                    A1[:, KU:KU + T], ALU.mult, ALU.add), (ev,))
            ev = p.dve(lambda e: e.tensor_scalar(
                A1[:, KU:KU + T], A1[:, KU:KU + T], -PI_LO, None, ALU.max), (ev,))
            ev = p.dve(lambda e: e.tensor_scalar(
                A1[:, KU:KU + T], A1[:, KU:KU + T], PI_LO, None, ALU.min), (ev,))
            ev_cos = p.act(lambda e: e.activation(
                A2[:, T:2 * T], A1[:, KU:KU + T], AF.Sin), (ev,))
            return (ev_sin, ev_cos)

        def emit_qkv(layer, rope_evs):
            wsrc = dr["W_qkv0"] if layer == 0 else dr["kq_dram"]
            bcol = BQKV0 if layer == 0 else BQKV1
            if layer == 0:
                ev_xnall = p.dma(
                    "gq", A1[:, 0:8 * T].rearrange("p (c t) -> p c t", t=T),
                    fmv(dr["xn_dram"]), rope_evs)
            else:
                ev_xnall = None
                if dbg2:
                    p.dma("gq", fmv(dr["xn_dram"]),
                          A1[:, 0:8 * T].rearrange("p (c t) -> p c t", t=T))
                    p.barrier()
            oc_order = list(range(8, 24)) + list(range(0, 8))
            w_hist = {}
            evac_hist = {}
            st_hist = {}
            rope_hist = {}
            ev_cc = None
            for idx, oc in enumerate(oc_order):
                wslot = (idx % 2) * 1024
                ww = (w_hist.get(idx - 2), ev_cc if False else None)
                evw = p.dma("sq",
                            A3[:, wslot:wslot + 1024].rearrange(
                                "p (c n) -> p c n", n=128),
                            wsrc.rearrange("(c p) n -> p c n",
                                           p=128)[:, :, oc * 128:(oc + 1) * 128],
                            (w_hist.get(idx - 2),))
                for tt in range(NTT):
                    i = idx * NTT + tt
                    pbk = 4 + i % 2
                    aslot = (i % 2) * TT

                    def gq_(e, wslot=wslot, tt=tt, pbk=pbk):
                        for c in range(NCH):
                            r = e.matmul(
                                ps[pbk][:, 0:TT],
                                A3[:, wslot + c * 128:wslot + c * 128 + 128],
                                A1[:, c * T + tt * TT:c * T + tt * TT + TT],
                                start=(c == 0), stop=(c == NCH - 1))
                        return r
                    evg = p.pe(gq_, (evw, ev_xnall, evac_hist.get(i - 2)))
                    w_hist[idx] = evg
                    ev_ev = p.act(lambda e, oc=oc, pbk=pbk, aslot=aslot:
                                  e.activation(
                                      A7[:, aslot:aslot + TT], ps[pbk][:, 0:TT],
                                      AF.Identity,
                                      bias=smalls[:, bcol + oc:bcol + oc + 1]),
                                  (evg, st_hist.get(i - 2)))
                    evac_hist[i] = ev_ev
                    src_slot = aslot
                    if layer == 0 and oc < 16:
                        evrot = p.pe(lambda e, aslot=aslot, pbk=pbk: e.matmul(
                            ps[6 + i % 2 if False else 6][:, 0:TT], permT,
                            A7[:, aslot:aslot + TT], start=True, stop=True),
                            (ev_ev, rope_hist.get(i - 1)) + rope_evs)
                        ev1 = p.dve(lambda e, aslot=aslot, tt=tt: e.tensor_tensor(
                            A7[:, 2 * TT:3 * TT], A7[:, aslot:aslot + TT],
                            A2[:, T + tt * TT:T + tt * TT + TT], ALU.mult),
                            (ev_ev, rope_hist.get(i - 1)))
                        ev2 = p.dve(lambda e, tt=tt: e.tensor_tensor(
                            A7[:, 3 * TT:4 * TT], ps[6][:, 0:TT],
                            A2[:, tt * TT:tt * TT + TT], ALU.mult),
                            (evrot, st_hist.get(i - 1)))
                        ev3 = p.dve(lambda e: e.tensor_tensor(
                            A7[:, 3 * TT:4 * TT], A7[:, 2 * TT:3 * TT],
                            A7[:, 3 * TT:4 * TT], ALU.add), (ev1, ev2))
                        rope_hist[i] = ev3
                        ev_done = ev3
                        src_slot = 3 * TT
                    else:
                        ev_done = ev_ev
                    lo = tt * TT
                    if oc < 8:
                        dst = dr["qq"][oc * 128:(oc + 1) * 128, lo:lo + TT]
                    elif oc < 16:
                        dst = kvr(dr["kv_snd"])[0, 0, oc - 8, :, lo:lo + TT]
                    else:
                        dst = kvr(dr["kv_snd"])[0, 1, oc - 16, :, lo:lo + TT]
                    st_hist[i] = p.dma("gq", dst,
                                       A7[:, src_slot:src_slot + TT], (ev_done,))
                if idx == 15:
                    ev_cc = p.step("gpsimd", lambda e: e.collective_compute(
                        "AllGather", ALU.bypass, replica_groups=REPL,
                        ins=[dr["kv_snd"]], outs=[dr["kv_all"]]),
                        (("gq", p.counts["gq"]),), "cc")
            p.barrier()
            return ev_cc

        def emit_flash(ev_cc):
            QR, KR, VR, VT = 0, T, 3 * T, 5 * T
            NS1, NS2, NS3 = 2 * T + 2 * TT, 2 * T + 3 * TT, 2 * T + 4 * TT
            last_pair_pe = None
            last_vt_pe = None
            last_pv = None
            last_store = None
            st_o = {}
            for pr in range(8):
                ld_q = p.dma("gq", A1[:, QR:QR + T],
                             dr["qq"][pr * 128:(pr + 1) * 128, :],
                             (last_pair_pe,))
                lds = [ld_q]
                for (reg, g) in ((KR, 0), (VR, 1)):
                    for r in range(2):
                        lds.append(p.dma(
                            "gq", A1[:, reg + r * T:reg + (r + 1) * T],
                            kvr(dr["kv_all"])[r, g, pr], (ev_cc, last_pair_pe)))
                ev_ms = p.dve(lambda e: e.memset(
                    A1[:, VT:VT + 2 * NKT * 65], 1.0), (last_pair_pe,))
                vt_evs = {}
                for hh in range(2):
                    hb = hh * 64
                    for kt in range(NKT):
                        j = hh * NKT + kt
                        bk = j % 2
                        evt = p.pe(lambda e, hb=hb, kt=kt, bk=bk: e.transpose(
                            ps[bk][:, 0:64],
                            A1[:, VR + kt * 128:VR + kt * 128 + 128][hb:hb + 64, :],
                            ident[hb:hb + 64, hb:hb + 64]),
                            (lds[3], lds[4], ev_ms, vt_evs.get(j - 2)))
                        vt_evs[j] = p.dve(lambda e, j=j, bk=bk: e.tensor_copy(
                            A1[:, VT + j * 65:VT + j * 65 + 64], ps[bk][:, 0:64]),
                            (evt,))
                last_vt = vt_evs[2 * NKT - 1]
                for hh in range(2):
                    hb = hh * 64
                    h = 2 * pr + hh
                    ev_pm = None
                    for ti in range(3 * NTT):
                        src = (QR + ti * TT) if ti < NTT else (KR + (ti - NTT) * TT)
                        ldw = (lds[0],) if ti < NTT else (lds[1], lds[2])
                        evq = p.act(lambda e, hb=hb, src=src: e.activation(
                            A2[hb:hb + 64, NS1:NS1 + TT],
                            A1[:, src:src + TT][hb:hb + 64, :], AF.Square),
                            ldw + (ev_pm, last_pv))
                        evn = p.pe(lambda e, hb=hb: e.matmul(
                            ps[2][0:1, 0:TT], ones[hb:hb + 64, 0:1],
                            A2[hb:hb + 64, NS1:NS1 + TT], start=True, stop=True),
                            (evq,))
                        ev_pm = p.dve(lambda e, ti=ti: e.tensor_reduce(
                            tiny[0:1, 70 + ti:71 + ti], ps[2][0:1, 0:TT],
                            AX.X, ALU.max), (evn,))
                    ev_qm = p.dve(lambda e: e.tensor_reduce(
                        tiny[0:1, 88:89], tiny[0:1, 70:70 + NTT], AX.X, ALU.max),
                        (ev_pm,))
                    ev_km = p.dve(lambda e: e.tensor_reduce(
                        tiny[0:1, 89:90], tiny[0:1, 70 + NTT:70 + 3 * NTT],
                        AX.X, ALU.max), (ev_pm,))
                    ev_s1 = p.act(lambda e: e.activation(
                        tiny[0:1, 88:89], tiny[0:1, 88:89], AF.Sqrt), (ev_qm,))
                    ev_s2 = p.act(lambda e: e.activation(
                        tiny[0:1, 89:90], tiny[0:1, 89:90], AF.Sqrt), (ev_km,))
                    ev_b = p.dve(lambda e: e.tensor_tensor(
                        tiny[0:1, 90:91], tiny[0:1, 88:89], tiny[0:1, 89:90],
                        ALU.mult), (ev_s1, ev_s2))
                    ev_b = p.dve(lambda e: e.tensor_scalar(
                        tiny[0:1, 90:91], tiny[0:1, 90:91], -0.125, None,
                        ALU.mult), (ev_b,))
                    ev_bb = p.pe(lambda e: e.matmul(
                        ps[3][:, 0:1], ones[0:1, :], tiny[0:1, 90:91],
                        start=True, stop=True), (ev_b, last_store))
                    ev_bc = p.dve(lambda e: e.tensor_copy(
                        tiny[:, 91:92], ps[3][:, 0:1]), (ev_bb,))
                    exp_h = {}
                    pv_h = {}
                    for qt in range(NTT):
                        prev_o = st_o.get((pr, hh, qt - 1))
                        for kt in range(NKT):
                            jg = qt * NKT + kt
                            sbk = 4 + kt % 2
                            evS = p.pe(lambda e, hb=hb, kt=kt, qt=qt, sbk=sbk:
                                       e.matmul(
                                           ps[sbk][:, 0:TT],
                                           A1[:, KR + kt * 128:KR + kt * 128 + 128][hb:hb + 64, :],
                                           A1[:, QR + qt * TT:QR + qt * TT + TT][hb:hb + 64, :],
                                           start=True, stop=True),
                                       (exp_h.get(jg - 2), last_vt, ev_bc))
                            evE = p.act(lambda e, kt=kt, sbk=sbk: e.activation(
                                A2[:, NS1 + (kt % 2) * TT:NS1 + (kt % 2) * TT + TT],
                                ps[sbk][:, 0:TT], AF.Exp, scale=0.125,
                                bias=tiny[:, 91:92]),
                                (evS, pv_h.get(jg - 2), prev_o))
                            exp_h[jg] = evE
                            pv_h[jg] = p.pe(
                                lambda e, j=hh * NKT + kt, kt=kt: e.matmul(
                                    ps[6][0:65, 0:TT],
                                    A1[:, VT + j * 65:VT + j * 65 + 65],
                                    A2[:, NS1 + (kt % 2) * TT:NS1 + (kt % 2) * TT + TT],
                                    start=(kt == 0), stop=(kt == NKT - 1)),
                                (exp_h[jg],))
                        last_pv = pv_h[qt * NKT + NKT - 1]
                        ev_rc = p.dve(lambda e: e.reciprocal(
                            A2[64:65, NS2:NS2 + TT], ps[6][64:65, 0:TT]),
                            (last_pv,))
                        ev_rb = p.pe(lambda e: e.matmul(
                            ps[3][0:64, 0:TT], ones[64:65, 0:64],
                            A2[64:65, NS2:NS2 + TT], start=True, stop=True),
                            (ev_rc,))
                        ev_rv = p.dve(lambda e: e.tensor_copy(
                            A2[0:64, NS2:NS2 + TT], ps[3][0:64, 0:TT]), (ev_rb,))
                        ev_o = p.dve(lambda e: e.tensor_tensor(
                            A2[0:64, NS3:NS3 + TT], ps[6][0:64, 0:TT],
                            A2[0:64, NS2:NS2 + TT], ALU.mult),
                            (ev_rv, st_o.get((pr, hh, qt - 1))))
                        st_o[(pr, hh, qt)] = p.dma(
                            "gq", dr["oo"][h * 64:h * 64 + 64,
                                           qt * TT:qt * TT + TT],
                            A2[0:64, NS3:NS3 + TT], (ev_o,))
                        last_store = st_o[(pr, hh, qt)]
                        last_pair_pe = last_pv
            p.barrier()

        def emit_outproj(layer):
            wsrc = dr["W_out0"] if layer == 0 else dr["ko_dram"]
            bcol = BOUT0 if layer == 0 else BOUT1
            if layer == 0:
                ev_oall = p.dma(
                    "gq", A1[:, 0:8 * T].rearrange("p (c t) -> p c t", t=T),
                    fmv(dr["oo"]))
            else:
                ev_oall = None
            w_hist = {}
            evac_hist = {}
            st_hist = {}
            add_hist = {}
            for oc in range(8):
                wslot = (oc % 2) * 1024
                evw = p.dma("sq",
                            A3[:, wslot:wslot + 1024].rearrange(
                                "p (c n) -> p c n", n=128),
                            wsrc.rearrange("(c p) n -> p c n",
                                           p=128)[:, :, oc * 128:(oc + 1) * 128],
                            (w_hist.get(oc - 2),))
                for tt in range(NTT):
                    i = oc * NTT + tt
                    pbk = 4 + i % 2
                    aslot = (i % 2) * TT

                    def go_(e, wslot=wslot, tt=tt, pbk=pbk):
                        for c in range(NCH):
                            r = e.matmul(
                                ps[pbk][:, 0:TT],
                                A3[:, wslot + c * 128:wslot + c * 128 + 128],
                                A1[:, c * T + tt * TT:c * T + tt * TT + TT],
                                start=(c == 0), stop=(c == NCH - 1))
                        return r
                    evg = p.pe(go_, (evw, ev_oall, evac_hist.get(i - 2)))
                    w_hist[oc] = evg
                    ev_ev = p.act(lambda e, oc=oc, pbk=pbk, aslot=aslot:
                                  e.activation(
                                      A7[:, aslot:aslot + TT], ps[pbk][:, 0:TT],
                                      AF.Identity,
                                      bias=smalls[:, bcol + oc:bcol + oc + 1]),
                                  (evg, st_hist.get(i - 2)))
                    evac_hist[i] = ev_ev
                    ev_x = p.dma("gq", A5[:, aslot:aslot + TT],
                                 dr["xfm"][oc * 128:(oc + 1) * 128,
                                           tt * TT:tt * TT + TT],
                                 (add_hist.get(i - 2),))
                    ev_a = p.dve(lambda e, aslot=aslot: e.tensor_tensor(
                        A7[:, aslot:aslot + TT], A7[:, aslot:aslot + TT],
                        A5[:, aslot:aslot + TT], ALU.add), (ev_ev, ev_x))
                    add_hist[i] = ev_a
                    st_hist[i] = p.dma("gq",
                                       dr["xfm"][oc * 128:(oc + 1) * 128,
                                                 tt * TT:tt * TT + TT],
                                       A7[:, aslot:aslot + TT], (ev_a,))
            p.barrier()

        def emit_l0_attn():
            emit_ln(pooled=False)
            rope_evs = emit_rope_tables()
            ev_cc = emit_qkv(0, rope_evs)
            emit_flash(ev_cc)
            emit_outproj(0)

        q_tile_lnev = [None]

        def q_tile(src, dst, W, waits):
            T1, T2, T3, T4 = 0, 1024, 2048, 3072
            ev = p.act(lambda e: e.activation(
                A4[:, T1:T1 + W], src, AF.Abs),
                tuple(waits) + (q_tile_lnev[0],))
            ev = p.dve(lambda e: e.tensor_scalar(
                A4[:, T1:T1 + W], A4[:, T1:T1 + W], 1e-35, None, ALU.max),
                (ev,))
            ev = p.act(lambda e: e.activation(
                A4[:, T2:T2 + W], A4[:, T1:T1 + W], AF.Ln), (ev,))
            q_tile_lnev[0] = ev
            ev = p.dve(lambda e: e.tensor_scalar(
                A4[:, T2:T2 + W], A4[:, T2:T2 + W], LOG2E, -0.5,
                ALU.mult, ALU.add), (ev,))
            ev = p.dve(lambda e: e.tensor_scalar(
                A4[:, T2:T2 + W], A4[:, T2:T2 + W], C_RNE, -C_RNE,
                ALU.add, ALU.add), (ev,))
            ev = p.dve(lambda e: e.tensor_scalar(
                A4[:, T2:T2 + W], A4[:, T2:T2 + W], LN2, None, ALU.mult), (ev,))
            evip = p.act(lambda e: e.activation(
                A4[:, T3:T3 + W], A4[:, T2:T2 + W], AF.Exp, scale=-1.0), (ev,))
            ev = p.act(lambda e: e.activation(
                A4[:, T2:T2 + W], A4[:, T2:T2 + W], AF.Exp), (evip,))
            ev = p.dve(lambda e: e.tensor_tensor(
                A4[:, T3:T3 + W], src, A4[:, T3:T3 + W], ALU.mult), (ev,))
            ev = p.dve(lambda e: e.tensor_scalar(
                A4[:, T3:T3 + W], A4[:, T3:T3 + W], 8.0, None, ALU.mult), (ev,))
            ev = p.dve(lambda e: e.tensor_scalar(
                A4[:, T3:T3 + W], A4[:, T3:T3 + W], C_RNE, -C_RNE,
                ALU.add, ALU.add), (ev,))
            ev = p.dve(lambda e: e.tensor_scalar(
                A4[:, T3:T3 + W], A4[:, T3:T3 + W], 0.125, None, ALU.mult), (ev,))
            ev = p.dve(lambda e: e.tensor_tensor(
                A4[:, T3:T3 + W], A4[:, T3:T3 + W], A4[:, T2:T2 + W], ALU.mult),
                (ev,))
            ev2 = p.dve(lambda e: e.tensor_scalar(
                A4[:, T4:T4 + W], src, 512.0, None, ALU.mult), tuple(waits) + (ev,))
            ev2 = p.dve(lambda e: e.tensor_scalar(
                A4[:, T4:T4 + W], A4[:, T4:T4 + W], C_RNE, -C_RNE,
                ALU.add, ALU.add), (ev2,))
            ev2 = p.dve(lambda e: e.tensor_scalar(
                A4[:, T4:T4 + W], A4[:, T4:T4 + W], 2.0 ** -9, None, ALU.mult),
                (ev2,))
            ev3 = p.dve(lambda e: e.tensor_scalar(
                A4[:, T1:T1 + W], A4[:, T1:T1 + W], 2.0 ** -6, None, ALU.is_ge),
                (ev2,))
            ev4 = p.dve(lambda e: e.tensor_tensor(
                A4[:, T3:T3 + W], A4[:, T3:T3 + W], A4[:, T4:T4 + W],
                ALU.subtract), (ev3,))
            ev4 = p.dve(lambda e: e.tensor_tensor(
                A4[:, T3:T3 + W], A4[:, T3:T3 + W], A4[:, T1:T1 + W],
                ALU.mult), (ev4,))
            ev4 = p.dve(lambda e: e.tensor_tensor(
                dst, A4[:, T3:T3 + W], A4[:, T4:T4 + W], ALU.add), (ev4,))
            return ev4

        def q_scale_div(src_t, dst_t, W, waits, scol):
            evs = []
            nseg = W // 128
            for sg in range(nseg):
                evr = p.dve(lambda e, sg=sg: e.tensor_reduce(
                    tiny[:, scol + sg:scol + sg + 1],
                    src_t[:, sg * 128:(sg + 1) * 128], AX.X, ALU.max,
                    apply_absolute_value=True), tuple(waits))
                evsc = p.dve(lambda e, sg=sg: e.tensor_scalar(
                    tiny[:, scol + sg:scol + sg + 1],
                    tiny[:, scol + sg:scol + sg + 1], 1.0 / 448.0, 1e-12,
                    ALU.mult, ALU.max), (evr,))
                evrc = p.dve(lambda e, sg=sg: e.reciprocal(
                    tiny[:, scol + 8 + sg:scol + 9 + sg],
                    tiny[:, scol + sg:scol + sg + 1]), (evsc,))
                evd = p.dve(lambda e, sg=sg: e.tensor_scalar(
                    dst_t[:, sg * 128:(sg + 1) * 128],
                    src_t[:, sg * 128:(sg + 1) * 128],
                    tiny[:, scol + 8 + sg:scol + 9 + sg], None, ALU.mult),
                    (evrc,))
                evs.append(evd)
            return evs

        def q_rescale(t, W, scol, waits):
            evs = []
            for sg in range(W // 128):
                evs.append(p.dve(lambda e, sg=sg: e.tensor_scalar(
                    t[:, sg * 128:(sg + 1) * 128], t[:, sg * 128:(sg + 1) * 128],
                    tiny[:, scol + sg:scol + sg + 1], None, ALU.mult),
                    tuple(waits)))
            return evs

        def emit_quant_kern(src, dst, NCOLS):
            prev_dev = None
            prev_st = None
            for kc in range(NCH):
                for cb in range(NCOLS // 1024):
                    evl = p.dma("gq", A5[:, 0:1024],
                                src[kc * 128:(kc + 1) * 128,
                                    cb * 1024:(cb + 1) * 1024], (prev_dev,))
                    devs = q_scale_div(A5, A6, 1024, (evl, prev_st), 0)
                    prev_dev = devs[-1]
                    evq = q_tile(A6[:, 0:1024], A6[:, 0:1024], 1024, devs)
                    revs = q_rescale(A6, 1024, 0, (evq,))
                    prev_st = p.dma("gq", dst[kc * 128:(kc + 1) * 128,
                                              cb * 1024:(cb + 1) * 1024],
                                    A6[:, 0:1024], tuple(revs))
            p.barrier()

        def emit_quant_fm_to_A1(src):
            tr_h = {}
            prev_b = None
            for jt in range(NJT):
                lds = {}
                for c in range(4):
                    lds[c] = p.dma(
                        "gq", A7[:, c * 128:c * 128 + 128],
                        src[c * 128:(c + 1) * 128, jt * 128:(jt + 1) * 128],
                        (tr_h.get(jt * NCH + c - 4),))
                t_evs = {}
                for c in range(NCH):
                    bk = c % 2
                    evt = p.pe(lambda e, c=c, bk=bk: e.transpose(
                        ps[bk][:, 0:128], A7[:, (c % 4) * 128:(c % 4) * 128 + 128],
                        ident), (lds[c], t_evs.get(c - 2), prev_b))
                    tr_h[jt * NCH + c] = evt
                    if c + 4 < NCH:
                        lds[c + 4] = p.dma(
                            "gq", A7[:, c * 128:c * 128 + 128],
                            src[(c + 4) * 128:(c + 5) * 128,
                                jt * 128:(jt + 1) * 128], (evt,))
                    t_evs[c] = p.dve(lambda e, c=c, bk=bk: e.tensor_copy(
                        A5[:, c * 128:c * 128 + 128], ps[bk][:, 0:128]), (evt,))
                devs = q_scale_div(A5, A6, 1024, (t_evs[NCH - 1], prev_b), 0)
                evq = q_tile(A6[:, 0:1024], A6[:, 0:1024], 1024, devs)
                revs = q_rescale(A6, 1024, 0, (evq,))
                b_evs = {}
                for c in range(NCH):
                    bk = c % 2
                    evt = p.pe(lambda e, c=c, bk=bk: e.transpose(
                        ps[bk][:, 0:128], A6[:, c * 128:c * 128 + 128], ident),
                        tuple(revs) + (b_evs.get(c - 2), t_evs.get(c + 2)))
                    ev_o = p.dve(lambda e, c=c, bk=bk, jt=jt: e.tensor_copy(
                        A1[:, c * T + jt * 128:c * T + jt * 128 + 128],
                        ps[bk][:, 0:128]), (evt,))
                    b_evs[c] = ev_o
                    if dbg:
                        p.dma("gq", dr["xn_dram"][c * 128:(c + 1) * 128,
                                                  jt * 128:(jt + 1) * 128],
                              A1[:, c * T + jt * 128:c * T + jt * 128 + 128],
                              (ev_o,))
                prev_b = b_evs[NCH - 1]
            p.barrier()

        def emit_l1_attn():
            emit_ln(pooled=False)
            emit_quant_kern(dr["W_qkv1"], dr["kq_dram"], 3 * D)
            emit_quant_kern(dr["W_out1"], dr["ko_dram"], D)
            emit_quant_fm_to_A1(dr["xn_dram"])
            ev_cc = emit_qkv(1, ())
            emit_flash(ev_cc)
            emit_quant_fm_to_A1(dr["oo"])
            emit_outproj(1)

        def emit_l1_ffn():
            emit_ln(pooled=False)
            prev_lge = None
            prev_evt = None
            prev_rmm = None
            for tt in range(NTT):
                lo = tt * TT
                ev_xn = p.dma("gq", A6[:, :].rearrange("p (c t) -> p c t", t=TT),
                              fmv(dr["xn_dram"])[:, :, lo:lo + TT], (prev_rmm,))

                def rmm(e):
                    for c in range(NCH):
                        r = e.matmul(ps[0][0:8, 0:TT],
                                     rwsb[:, c * NE:(c + 1) * NE],
                                     A6[:, c * TT:(c + 1) * TT],
                                     start=(c == 0), stop=(c == NCH - 1))
                    return r
                ev_lg = p.pe(rmm, (ev_xn, prev_lge))
                prev_rmm = ev_lg
                ev_lge = p.act(lambda e: e.activation(
                    A7[0:8, 0:TT], ps[0][0:8, 0:TT], AF.Identity,
                    bias=rbsb[:, 0:1]), (ev_lg, prev_evt))
                prev_lge = ev_lge
                last_wfm = None
                for jb in range(TT // 128):
                    evt = p.pe(lambda e, jb=jb: e.transpose(
                        ps[1][:, 0:8], A7[0:8, jb * 128:(jb + 1) * 128],
                        ident[0:8, 0:8]), (ev_lge, last_wfm))
                    prev_evt = evt
                    ev = p.dve(lambda e: e.tensor_copy(
                        A7[:, TT:TT + 8], ps[1][:, 0:8]), (evt,))
                    ev = p.dve(lambda e: e.tensor_reduce(
                        A7[:, TT + 8:TT + 9], A7[:, TT:TT + 8], AX.X, ALU.max),
                        (ev,))
                    ev = p.dve(lambda e: e.tensor_scalar(
                        A7[:, TT + 8:TT + 9], A7[:, TT + 8:TT + 9], -1.0, None,
                        ALU.mult), (ev,))
                    ev = p.act(lambda e: e.activation(
                        A7[:, TT:TT + 8], A7[:, TT:TT + 8], AF.Exp,
                        bias=A7[:, TT + 8:TT + 9]), (ev,))
                    ev = p.dve(lambda e: e.tensor_reduce(
                        A7[:, TT + 8:TT + 9], A7[:, TT:TT + 8], AX.X, ALU.add),
                        (ev,))
                    ev = p.dve(lambda e: e.reciprocal(
                        A7[:, TT + 8:TT + 9], A7[:, TT + 8:TT + 9]), (ev,))
                    ev = p.dve(lambda e: e.tensor_scalar(
                        A7[:, TT:TT + 8], A7[:, TT:TT + 8],
                        A7[:, TT + 8:TT + 9], None, ALU.mult), (ev,))
                    ev = p.dve(lambda e: e.tensor_reduce(
                        A7[:, TT + 8:TT + 9], A7[:, TT:TT + 8], AX.X, ALU.max),
                        (ev,))
                    ev = p.dve(lambda e: e.scalar_tensor_tensor(
                        A7[:, TT + 16:TT + 24], A7[:, TT:TT + 8],
                        A7[:, TT + 8:TT + 9], A7[:, TT:TT + 8],
                        ALU.is_lt, ALU.mult), (ev,))
                    ev = p.dve(lambda e: e.tensor_reduce(
                        A7[:, TT + 8:TT + 9], A7[:, TT + 16:TT + 24], AX.X,
                        ALU.max), (ev,))
                    ev = p.dve(lambda e: e.tensor_scalar(
                        A7[:, TT + 16:TT + 24], A7[:, TT:TT + 8],
                        A7[:, TT + 8:TT + 9], None, ALU.is_ge), (ev,))
                    ev = p.dve(lambda e: e.tensor_tensor(
                        A7[:, TT + 16:TT + 24], A7[:, TT + 16:TT + 24],
                        A7[:, TT:TT + 8], ALU.mult), (ev,))
                    evt2 = p.pe(lambda e, jb=jb: e.transpose(
                        ps[1][0:8, 0:128], A7[:, TT + 16:TT + 24],
                        ident), (ev,))
                    last_wfm = p.dve(lambda e, jb=jb, lo=lo: e.tensor_copy(
                        wfm[:, lo + jb * 128:lo + jb * 128 + 128],
                        ps[1][0:8, 0:128]), (evt2,))
            if dbg3:
                p.dma("gq", dr["xn_dram"][0:8, :], wfm[:, :], (last_wfm,))
            p.barrier()

            NNB1 = H1 // 256
            for tt in range(NTT):
                lo = tt * TT
                ev_xn = p.dma("gq", A6[:, :].rearrange("p (c t) -> p c t", t=TT),
                              fmv(dr["xn_dram"])[:, :, lo:lo + TT])
                ev_x = p.dma("gq", A5[:, :].rearrange("p (c t) -> p c t", t=TT),
                             fmv(dr["xfm"])[:, :, lo:lo + TT])
                ev_z = {}
                gelu_evs = [None] * (H1 // 128)
                g1_evs = {}
                g2_evs = {}
                wb_evs = {}
                for e_i in range(9):
                    shared = (e_i == 8)
                    if not shared:
                        ev_wb = p.pe(lambda e, e_i=e_i, lo=lo: e.matmul(
                            ps[3][:, 0:TT],
                            ident[0:8, e_i:e_i + 1].to_broadcast((8, 128)),
                            wfm[:, lo:lo + TT],
                            start=True, stop=True), (wb_evs.get(e_i - 1),))
                        ev_wc = p.dve(lambda e: e.tensor_copy(
                            A7[:, 3 * TT:4 * TT], ps[3][:, 0:TT]), (ev_wb,))
                        wb_evs[e_i] = ev_wc
                    for nb in range(NNB1):
                        off = (nb % 2) * WACOLS
                        if (e_i, nb - 2) in g1_evs:
                            ww = (g1_evs[(e_i, nb - 2)],)
                        elif e_i > 0:
                            ww = (g1_evs[(e_i - 1, NNB1 - 2 + nb)],)
                        else:
                            ww = (ev_xn,)
                        wsrc = (dr["sw1"] if shared else dr["rw1"][e_i])
                        evw = p.dma("sq",
                                    A3[:, off:off + WACOLS].rearrange(
                                        "p (c n) -> p c n", n=256),
                                    wsrc.rearrange(
                                        "(c p) n -> p c n",
                                        p=128)[:, :, nb * 256:(nb + 1) * 256],
                                    ww)
                        for nn in range(2):
                            n = nb * 2 + nn
                            pbk = 4 + n % 2

                            def g1(e, nn=nn, off=off, pbk=pbk):
                                for c in range(NCH):
                                    r = e.matmul(
                                        ps[pbk][:, 0:TT],
                                        A3[:, off + c * 256 + nn * 128:
                                           off + c * 256 + nn * 128 + 128],
                                        A6[:, c * TT:(c + 1) * TT],
                                        start=(c == 0), stop=(c == NCH - 1))
                                return r
                            evg = p.pe(g1, (evw, ev_xn, gelu_evs[n - 2]))
                            g1_evs[(e_i, nb)] = evg
                            bias_ap = (smalls[:, SB1 + n:SB1 + n + 1] if shared
                                       else smalls[:, RB1 + e_i * 16 + n:
                                                   RB1 + e_i * 16 + n + 1])
                            if shared:
                                gelu_evs[n] = p.act(
                                    lambda e, n=n, pbk=pbk, b=bias_ap:
                                    e.activation(
                                        A1[:, n * TT:(n + 1) * TT],
                                        ps[pbk][:, 0:TT], AF.Gelu, bias=b),
                                    (evg,))
                            else:
                                ev_gl = p.act(
                                    lambda e, n=n, pbk=pbk, b=bias_ap:
                                    e.activation(
                                        A7[:, (n % 2) * TT:(n % 2) * TT + TT],
                                        ps[pbk][:, 0:TT], AF.Gelu, bias=b),
                                    (evg, gelu_evs[n - 2]))
                                gelu_evs[n] = p.dve(
                                    lambda e, n=n: e.tensor_tensor(
                                        A1[:, n * TT:(n + 1) * TT],
                                        A7[:, (n % 2) * TT:(n % 2) * TT + TT],
                                        A7[:, 3 * TT:4 * TT], ALU.mult),
                                    (ev_gl, wb_evs[e_i]))
                    for d_i in range(8):
                        off = (d_i % 2) * WBCOLS
                        if (e_i, d_i - 2) in g2_evs:
                            ww = (g2_evs[(e_i, d_i - 2)],)
                        elif e_i > 0:
                            ww = (g2_evs[(e_i - 1, 6 + d_i)],)
                        else:
                            ww = ()
                        wsrc2 = (dr["sw2"] if shared else dr["rw2"][e_i])
                        evw = p.dma("sq",
                                    A2[:, off:off + (H1 // 128) * 128].rearrange(
                                        "p (c n) -> p c n", n=128),
                                    wsrc2.rearrange(
                                        "(c p) n -> p c n",
                                        p=128)[:, :, d_i * 128:(d_i + 1) * 128],
                                    ww)
                        pbk = 6 + d_i % 2

                        def g2(e, off=off, pbk=pbk, e_i=e_i, d_i=d_i, lo=lo):
                            nlast = H1 // 128 - 1
                            for n2 in range(H1 // 128):
                                r = e.matmul(
                                    ps[pbk][:, 0:TT],
                                    A2[:, off + n2 * 128:off + n2 * 128 + 128],
                                    A1[:, n2 * TT:(n2 + 1) * TT],
                                    start=(n2 == 0),
                                    stop=(e_i != 0 and n2 == nlast))
                            if e_i == 0:
                                r = e.matmul(
                                    ps[pbk][:, 0:TT],
                                    rb2sb[0:8, d_i * 128:(d_i + 1) * 128],
                                    wfm[:, lo:lo + TT], start=False, stop=True)
                            return r
                        evg2 = p.pe(g2, (evw, gelu_evs[H1 // 128 - 1],
                                         ev_z.get(d_i - 2),
                                         ev_z.get(6 + d_i) if e_i > 0 and d_i < 2
                                         else None))
                        g2_evs[(e_i, d_i)] = evg2
                        if e_i == 0:
                            ev_z[d_i] = p.dve(
                                lambda e, d=d_i, pbk=pbk: e.tensor_copy(
                                    A4[:, d * TT:d * TT + TT], ps[pbk][:, 0:TT]),
                                (evg2,))
                        elif shared:
                            ev_z[d_i] = p.dve(
                                lambda e, d=d_i, pbk=pbk:
                                e.scalar_tensor_tensor(
                                    A4[:, d * TT:d * TT + TT], ps[pbk][:, 0:TT],
                                    smalls[:, SB2 + d:SB2 + d + 1],
                                    A4[:, d * TT:d * TT + TT],
                                    ALU.add, ALU.add), (evg2, ev_z.get(d_i)))
                        else:
                            ev_z[d_i] = p.dve(
                                lambda e, d=d_i, pbk=pbk: e.tensor_tensor(
                                    A4[:, d * TT:d * TT + TT],
                                    A4[:, d * TT:d * TT + TT],
                                    ps[pbk][:, 0:TT], ALU.add),
                                (evg2, ev_z.get(d_i)))
                for c in range(NCH):
                    ev = p.dve(lambda e, c=c: e.scalar_tensor_tensor(
                        A5[:, c * TT:(c + 1) * TT],
                        A4[:, c * TT:c * TT + TT], 0.5,
                        A5[:, c * TT:(c + 1) * TT], ALU.mult, ALU.add),
                        (ev_z[c], ev_x))
                    p.dma("gq", dr["xfm"][c * 128:(c + 1) * 128, lo:lo + TT],
                          A5[:, c * TT:(c + 1) * TT], (ev,))
                p.barrier()

        def dbg_ln_only():
            emit_ln(pooled=True)

        def dbg_ln_nopool():
            emit_ln(pooled=False)

        def dbg_quant_only():
            emit_ln(pooled=False)
            emit_quant_fm_to_A1(dr["xn_dram"])
        def dbg_quantk_only():
            emit_quant_kern(dr["W_qkv1"], dr["kq_dram"], 3 * D)
        def dbg_l1_prefix():
            emit_ln(pooled=False)
            emit_quant_kern(dr["W_qkv1"], dr["kq_dram"], 3 * D)
            emit_quant_kern(dr["W_out1"], dr["ko_dram"], D)
            emit_quant_fm_to_A1(dr["xn_dram"])
        def dbg_l1_qkv():
            dbg_l1_prefix()
            emit_qkv(1, ())
        def dbg_kq_slice():
            dbg_quantk_only()
            p.dma("gq", dr["xn_dram"][:, :], dr["kq_dram"][:, 1024:1024 + T])
            p.barrier()
        def dbg_l0_router():
            emit_l0_ffn_router_only[0] = True
            try:
                emit_l0_ffn()
            finally:
                emit_l0_ffn_router_only[0] = False
        SUBS = [emit_l0_ffn, emit_l0_attn, emit_l0_ffn,
                emit_l1_ffn, emit_l1_attn, emit_l1_ffn, dbg_quant_only,
                dbg_quantk_only, dbg_l1_prefix, dbg_l1_qkv, dbg_kq_slice,
                dbg_ln_only, dbg_l0_router, dbg_ln_nopool]
        if subsel is None:
            subsel = list(range(min(nsub, len(SUBS))))
        for i in subsel:
            SUBS[i]()

        p.barrier()
        ev_ya = ev_yb = None
        evv_h = {}
        for jt in range(NJT):
            lds = [p.dma("gq", A1[:, c * 128:c * 128 + 128],
                         dr[out_src][c * 128:(c + 1) * 128,
                                     jt * 128:(jt + 1) * 128],
                         (evv_h.get((jt - 1) * NCH + c),))
                   for c in range(NCH)]
            for c in range(NCH):
                bk = c % 2
                g = jt * NCH + c
                evt = p.pe(lambda e, c=c, bk=bk: e.transpose(
                    ps[bk][:, 0:128], A1[:, c * 128:c * 128 + 128], ident),
                    (lds[c], evv_h.get(g - 2)))
                evv = p.dve(lambda e, c=c, bk=bk: e.tensor_copy(
                    A7[:, c * 128:c * 128 + 128] if c < 4 else
                    A6[:, (c - 4) * 128:(c - 4) * 128 + 128], ps[bk][:, 0:128]),
                    (evt, ev_ya if c < 4 else ev_yb))
                evv_h[g] = evv
                if c == NCH - 1:
                    ev_last = evv
            ev_ya = p.dma("gq", dr["y"][jt * 128:(jt + 1) * 128, 0:512],
                          A7[:, 0:512], (ev_last,))
            ev_yb = p.dma("gq", dr["y"][jt * 128:(jt + 1) * 128, 512:1024],
                          A6[:, 0:512], (ev_last,))
        p.final_waits["gpsimd"] = (("gq", p.counts["gq"]),)

        p.emit(block)
    return nc


_NC_CACHE = {}


def kernel(**inputs):
    T = 2048
    if T not in _NC_CACHE:
        _NC_CACHE[T] = build_program(T)
    nc = _NC_CACHE[T]

    x = np.asarray(inputs["L0_attn_x"])
    B, S, _ = x.shape
    permT64 = np.zeros((64, 64), dtype=np.float32)
    for i in range(32):
        permT64[2 * i + 1, i] = -1.0
        permT64[2 * i, 32 + i] = 1.0
    permT = np.zeros((128, 128), dtype=np.float32)
    permT[0:64, 0:64] = permT64
    permT[64:128, 64:128] = permT64

    common = {
        "W_qkv0": inputs["L0_attn_Wqkv"], "b_qkv0": inputs["L0_attn_bqkv"],
        "W_out0": inputs["L0_attn_Wout"], "b_out0": inputs["L0_attn_bout"],
        "W1r": inputs["L0_ffn_W1"], "b1r": inputs["L0_ffn_b1"],
        "W2r": inputs["L0_ffn_W2"], "b2r": inputs["L0_ffn_b2"],
        "Wa": inputs["L0_ffn_Wa"], "Wb": inputs["L0_ffn_Wb"],
        "W_qkv1": inputs["L1_attn_qkv_kernel"], "b_qkv1": inputs["L1_attn_qkv_bias"],
        "W_out1": inputs["L1_attn_out_kernel"], "b_out1": inputs["L1_attn_out_bias"],
        "rt_w": inputs["L1_ffn_router_w"], "rt_b": inputs["L1_ffn_router_b"],
        "sw1": inputs["L1_ffn_sw1"], "sb1": inputs["L1_ffn_sb1"],
        "sw2": inputs["L1_ffn_sw2"], "sb2": inputs["L1_ffn_sb2"],
        "rw1": inputs["L1_ffn_rw1"], "rb1": inputs["L1_ffn_rb1"],
        "rw2": inputs["L1_ffn_rw2"], "rb2": inputs["L1_ffn_rb2"],
        "c_ident": np.eye(128, dtype=np.float32),
        "c_ones": np.ones((128, 128), dtype=np.float32),
        "c_permT": permT,
    }
    common = {k: np.ascontiguousarray(np.asarray(v, dtype=np.float32))
              for k, v in common.items()}
    in_maps = []
    for c in range(8):
        b, half = c // 2, c % 2
        m = dict(common)
        m["x_in"] = np.ascontiguousarray(x[b, half * T:(half + 1) * T, :])
        m["c_toff"] = np.array([[half * T]], dtype=np.float32)
        in_maps.append(m)
    res = run_bass_kernel_spmd(nc, in_maps, core_ids=list(range(8)))
    out = np.empty((B, S, D), dtype=np.float32)
    for c in range(8):
        b, half = c // 2, c % 2
        out[b, half * T:(half + 1) * T, :] = res.results[c]["y"]
    return out
